# version 02
# baseline (speedup 1.0000x reference)
import numpy as np
import ml_dtypes
from contextlib import ExitStack

import concourse.bass as bass
import concourse.mybir as mybir
from concourse.bass_utils import run_bass_kernel_spmd

F32 = mybir.dt.float32
BF16 = mybir.dt.bfloat16
I32 = mybir.dt.int32
AX = mybir.AxisListType.X
ALU = mybir.AluOpType
ACT = mybir.ActivationFunctionType

T = 2048
D = 1024
NCH = 16
CT = 8


class Emit:
    def __init__(self, nc, stack):
        self.nc = nc
        self.stack = stack
        self.c = {"T": 0, "V": 0, "S": 0, "DH": 0, "DG": 0}
        self.sems = {k: stack.enter_context(nc.semaphore(k)) for k in self.c}
        self.t = nc.tensor
        self.v = nc.vector
        self.s = nc.scalar
        self.q = nc.sync
        self.g = nc.gpsimd

    def sb(self, name, shape, dt):
        return self.stack.enter_context(self.nc.sbuf_tensor(name, shape, dt))

    def inc(self, instr, sem, amt=1):
        self.c[sem] += amt
        instr.then_inc(self.sems[sem], amt)
        return self.c[sem]

    def vchain(self, instr, amt=1):
        c = self.inc(instr, "V", amt)
        self.v.wait_ge(self.sems["V"], c)
        return c

    def wait(self, eng, sem, val=None):
        v = self.c[sem] if val is None else val
        if v > 0:
            eng.wait_ge(self.sems[sem], v)

    def barrier(self):
        for eng in (self.t, self.v, self.s, self.q, self.g):
            for sem in self.c:
                self.wait(eng, sem)



def trans_tiles(e, pairs, PST0, PST1, ident):
    t, v = e.t, e.v
    last_v = [0, 0]
    for idx, (dst, src) in enumerate(pairs):
        bk = idx % 2
        ps = PST0 if bk == 0 else PST1
        if last_v[bk]:
            e.wait(t, "V", last_v[bk])
        m, n = src.shape[0], src.shape[1]
        tk = e.inc(t.transpose(ps[0:n, 0:m], src, ident[0:m, 0:m]), "T")
        e.wait(v, "T", tk)
        last_v[bk] = e.inc(v.tensor_copy(dst, ps[0:n, 0:m]), "V")
    e.wait(t, "V")


def bcast_row(e, dst_sbuf, dram_row, ROWF, PSB, ONES1F, n_tot, scale=None):
    t, v, q = e.t, e.v, e.q
    for po in range(0, n_tot, 1024):
        pn = min(1024, n_tot - po)
        e.wait(q, "T")
        dk = e.inc(q.dma_start(out=ROWF[0:1, 0:pn], in_=dram_row[None, po:po + pn]), "DH", 16)
        e.wait(t, "DH", dk)
        for off in range(0, pn, 512):
            n = min(512, pn - off)
            tk = e.inc(t.matmul(PSB[:, 0:n], ONES1F[0:1, :], ROWF[0:1, off:off + n],
                                start=True, stop=True), "T")
            e.wait(v, "T", tk)
            if scale is None:
                vv = e.inc(v.tensor_copy(dst_sbuf[:, po + off:po + off + n], PSB[:, 0:n]), "V")
            else:
                vv = e.inc(v.tensor_scalar(dst_sbuf[:, po + off:po + off + n], PSB[:, 0:n],
                                           scale, None, ALU.mult), "V")
            e.wait(t, "V", vv)


def emit_ln(e, X, NEGM, RSTD, JUNKF, SS, EPSC):
    t, v, s = e.t, e.v, e.s
    for i in range(NCH):
        xc = X[:, i * D:(i + 1) * D]
        e.vchain(v.reduce_sum(SS[:, 2:3], xc, axis=AX))
        vv = e.inc(v.tensor_scalar(NEGM[:, i:i + 1], SS[:, 2:3], -1.0 / D, None, ALU.mult), "V")
        e.wait(s, "V", vv)
        e.inc(s.activation(JUNKF[:, 0:D], xc, ACT.Square, bias=NEGM[:, i:i + 1],
                           scale=1.0, accum_out=SS[:, 0:1]), "S")
        sk = e.inc(s.activation(SS[:, 1:2], SS[:, 0:1], ACT.Sqrt,
                                bias=EPSC[:, 0:1], scale=1.0 / D), "S")
        e.wait(v, "S", sk)
        e.vchain(v.reciprocal(RSTD[:, i:i + 1], SS[:, 1:2]))


def emit_xnt(e, X, XNT, NEGM, RSTD, XNC, PST0, PST1, IDBF):
    t, v = e.t, e.v
    for i in range(NCH):
        xc = X[:, i * D:(i + 1) * D]
        vv = e.inc(v.tensor_scalar(XNC[:], xc, NEGM[:, i:i + 1], RSTD[:, i:i + 1],
                                   ALU.add, ALU.mult), "V")
        e.wait(t, "V", vv)
        pairs = [(XNT[:, c * T + i * 128: c * T + (i + 1) * 128],
                  XNC[:, c * 128:(c + 1) * 128]) for c in range(CT)]
        trans_tiles(e, pairs, PST0, PST1, IDBF)


def load_wT(e, wdram, WROW, WT, PST0, PST1, IDBF, out_rows, in_dim=D):
    t, v, g = e.t, e.v, e.g
    nb = out_rows // 128
    for b in range(nb):
        e.wait(g, "V")
        dk = e.inc(g.dma_start(out=WROW[:, 0:in_dim], in_=wdram[b * 128:(b + 1) * 128, :]), "DG", 16)
        e.wait(t, "DG", dk)
        pairs = []
        for c in range(in_dim // 128):
            pairs.append((WT[:, c * out_rows + b * 128: c * out_rows + (b + 1) * 128],
                          WROW[:, c * 128:(c + 1) * 128]))
        trans_tiles(e, pairs, PST0, PST1, IDBF)


def projT(e, dst, lhsT_tiles, src_cols, PSA, bias_ap=None, n_tot=T):
    t, v = e.t, e.v
    last_v = [0, 0]
    nk = len(lhsT_tiles)
    mrows = dst.shape[0]
    for bi, off in enumerate(range(0, n_tot, 512)):
        n = min(512, n_tot - off)
        bk = bi % 2
        pb = PSA[0:mrows, bk * 512: bk * 512 + n]
        if last_v[bk]:
            e.wait(t, "V", last_v[bk])
        for k in range(nk):
            mm = t.matmul(pb, lhsT_tiles[k], src_cols(k, off, n),
                          start=(k == 0), stop=(k == nk - 1))
        tk = e.inc(mm, "T")
        e.wait(v, "T", tk)
        if bias_ap is not None:
            cp = v.tensor_scalar(dst[:, off:off + n], pb, bias_ap, None, ALU.add)
        else:
            cp = v.tensor_copy(dst[:, off:off + n], pb)
        last_v[bk] = e.inc(cp, "V")
    e.wait(t, "V")



def build_nc(n_layers=4, dbg=None):
    nc = bass.Bass("TRN2", target_bir_lowering=False)

    dx = nc.dram_tensor("x_in", [T, D], F32, kind="ExternalInput").ap()
    W = {}

    def din(name, shape):
        W[name] = nc.dram_tensor(name, list(shape), F32, kind="ExternalInput").ap()

    for nm in ("Wq", "Wk", "Wv", "Wo"):
        din("L0_attn_" + nm, (D, D))
    for nm in ("bq", "bk", "bv", "bo"):
        din("L0_attn_" + nm, (D,))
    din("L0_ffn_w1", (4096, D)); din("L0_ffn_b1", (4096,))
    din("L0_ffn_w2", (D, 4096)); din("L0_ffn_b2", (D,))
    din("L0_ffn_gamma", (D,)); din("L0_ffn_beta", (D,))
    din("L1_attn_W_dq", (512, D)); din("L1_attn_W_uq", (D, 512))
    din("L1_attn_W_dkv", (512, D)); din("L1_attn_W_uk", (D, 512))
    din("L1_attn_W_uv", (D, 512)); din("L1_attn_W_o", (D, D))
    din("L1_ffn_Wg", (D, 8)); din("L1_ffn_bg", (8,))
    din("L1_ffn_W1", (8, D, D)); din("L1_ffn_b1", (8, D))
    din("L1_ffn_W2", (8, D, D)); din("L1_ffn_b2", (8, D))
    din("L2_attn_gamma", (1, 1, D)); din("L2_attn_beta", (1, 1, D))
    din("L2_attn_w_qkv", (D, 384)); din("L2_attn_w_out", (128, D))
    din("L2_attn_b_out", (D,))
    din("L2_ffn_weight", (D, D)); din("L2_ffn_aux_weight1", (D, D))
    din("L2_ffn_aux_weight2", (D, D)); din("L2_ffn_bias", (D,))
    for nm in ("Wq", "Wk", "Wv", "Wo"):
        din("L3_attn_" + nm, (D, D))
    din("L3_attn_bo", (D,))
    din("L3_ffn_router_w", (D, 8)); din("L3_ffn_router_b", (8,))
    din("L3_ffn_w1", (8, D, 8192)); din("L3_ffn_w2", (8, 4096, D))

    c_idbf = nc.dram_tensor("c_idbf", [128, 128], BF16, kind="ExternalInput").ap()
    c_idf = nc.dram_tensor("c_idf", [128, 128], F32, kind="ExternalInput").ap()
    c_masku = nc.dram_tensor("c_masku", [128, 128], F32, kind="ExternalInput").ap()
    c_sel16 = nc.dram_tensor("c_sel16", [16, 2048], F32, kind="ExternalInput").ap()

    w1qt_scr = nc.dram_tensor("w1qt_scr", [128, 32 * D], BF16, kind="Internal").ap()
    w2qt_scr = nc.dram_tensor("w2qt_scr", [128, 32 * D], BF16, kind="Internal").ap()
    w1scr3 = nc.dram_tensor("w1scr3", [128, 8 * 65536], BF16, kind="Internal").ap()
    w2scr3 = nc.dram_tensor("w2scr3", [128, 8 * 32768], BF16, kind="Internal").ap()

    dout = nc.dram_tensor("out", [T, D], F32, kind="ExternalOutput").ap()
    ddbg = nc.dram_tensor("dbg", [128, CT * T], F32, kind="ExternalOutput").ap() if dbg else None

    with ExitStack() as stack:
        e = Emit(nc, stack)
        e.w1scr3 = w1scr3
        e.w2scr3 = w2scr3
        t, v, s, q, g = e.t, e.v, e.s, e.q, e.g

        X = e.sb("X", [128, NCH * D], F32)
        XNT = e.sb("XNT", [128, CT * T], BF16)
        NEGM = e.sb("NEGM", [128, NCH], F32)
        RSTD = e.sb("RSTD", [128, NCH], F32)
        SS = e.sb("SS", [128, 4], F32)
        JUNKF = e.sb("JUNKF", [128, D], F32)
        XNC = e.sb("XNC", [128, D], BF16)
        IDBF = e.sb("IDBF", [128, 128], BF16)
        IDF = e.sb("IDF", [128, 128], F32)
        MASKU = e.sb("MASKU", [128, 128], F32)
        MASKUB = e.sb("MASKUB", [128, 128], BF16)
        ONES1F = e.sb("ONES1F", [1, 128], F32)
        ONESCB = e.sb("ONESCB", [128, 1], BF16)
        ONESCF = e.sb("ONESCF", [128, 1], F32)
        EPSC = e.sb("EPSC", [128, 1], F32)
        EPS9 = e.sb("EPS9", [128, 1], F32)

        block = stack.enter_context(nc.Block())

        @block.sync
        def _(sync):
            xr = dx.rearrange("(i p) c -> p i c", p=128)
            Xr = X[:].rearrange("p (i c) -> p i c", c=D)
            e.inc(q.dma_start(out=Xr, in_=xr), "DH", 16)
            e.inc(q.dma_start(out=IDBF[:], in_=c_idbf), "DH", 16)
            e.inc(q.dma_start(out=IDF[:], in_=c_idf), "DH", 16)
            e.inc(q.dma_start(out=MASKU[:], in_=c_masku), "DH", 16)
            e.wait(v, "DH")
            e.inc(v.tensor_copy(MASKUB[:], MASKU[:]), "V")
            e.inc(v.memset(ONES1F[:], 1.0), "V")
            e.inc(v.memset(ONESCB[:], 1.0), "V")
            e.inc(v.memset(ONESCF[:], 1.0), "V")
            e.inc(v.memset(EPSC[:], 1e-5), "V")
            e.inc(v.memset(EPS9[:], 1e-9), "V")
            e.barrier()

            common = dict(X=X, XNT=XNT, NEGM=NEGM, RSTD=RSTD, SS=SS,
                          JUNKF=JUNKF, XNC=XNC, IDBF=IDBF, IDF=IDF,
                          MASKU=MASKU, MASKUB=MASKUB, c_sel16=c_sel16,
                          ONES1F=ONES1F, ONESCB=ONESCB, ONESCF=ONESCF,
                          EPSC=EPSC, EPS9=EPS9, ROWF=JUNKF, W=W,
                          w1qt_scr=w1qt_scr, w2qt_scr=w2qt_scr,
                          dbg=dbg, ddbg=ddbg)
            layers = [emit_layer0, emit_layer1, emit_layer2, emit_layer3]
            for li in range(n_layers):
                layers[li](e, **common)
                e.barrier()

            outr = dout.rearrange("(i p) c -> p i c", p=128)
            e.inc(q.dma_start(out=outr, in_=Xr), "DH", 16)
            e.wait(q, "DH")
    return nc



def emit_layer0(e, X, XNT, NEGM, RSTD, SS, JUNKF, XNC, IDBF, IDF, MASKU, MASKUB,
                c_sel16, ONES1F, ONESCB, ONESCF, EPSC, EPS9, ROWF, W, w1qt_scr, w2qt_scr,
                dbg=None, ddbg=None):
    nc = e.nc
    t, v, s, q, g = e.t, e.v, e.s, e.q, e.g

    emit_ln(e, X, NEGM, RSTD, JUNKF, SS, EPSC)
    e.barrier()
    with ExitStack() as st:
        PST0 = st.enter_context(nc.psum_tensor("l0_PST0", [128, 128], BF16))
        PST1 = st.enter_context(nc.psum_tensor("l0_PST1", [128, 128], BF16))
        emit_xnt(e, X, XNT, NEGM, RSTD, XNC, PST0, PST1, IDBF)
    e.barrier()
    if dbg == "stats":
        with nc.sbuf_tensor("dbgf", [1, 1], F32):
            e.inc(q.dma_start(out=ddbg[:, 0:NCH], in_=NEGM[:]), "DH", 16)
            e.inc(q.dma_start(out=ddbg[:, 16:16 + NCH], in_=RSTD[:]), "DH", 16)
        e.barrier()
        return
    if dbg == "xnt":
        with nc.sbuf_tensor("dbgf", [128, CT * T], F32) as DBF:
            for c in range(CT * T // 1024):
                e.inc(e.v.tensor_copy(DBF[:, c * 1024:(c + 1) * 1024],
                                      XNT[:, c * 1024:(c + 1) * 1024]), "V")
            e.barrier()
            e.inc(q.dma_start(out=ddbg, in_=DBF[:]), "DH", 16)
        e.barrier()
        return

    with ExitStack() as st:
        def sb(n, sh, d):
            return st.enter_context(nc.sbuf_tensor(n, sh, d))
        GB = sb("f0_GB", [128, D], F32)
        BB0 = sb("f0_BB0", [128, D], F32)
        B1B = sb("f0_B1B", [128, 4096], BF16)
        B2B = sb("f0_B2B", [128, D], F32)
        SC = sb("f0_SC", [128, 12], F32)
        SM = sb("f0_SM", [1, 2], F32)
        SWB = sb("f0_SWB", [128, 4], F32)
        PSA = st.enter_context(nc.psum_tensor("f0_PSA", [128, 1024], F32))
        PSC = st.enter_context(nc.psum_tensor("f0_PSC", [128, 1024], F32))
        PST0 = st.enter_context(nc.psum_tensor("f0_PST0", [128, 128], BF16))
        PST1 = st.enter_context(nc.psum_tensor("f0_PST1", [128, 128], BF16))

        w1d = W["L0_ffn_w1"]
        w2d = W["L0_ffn_w2"]
        w1blocks = [w1d[b * 512:(b + 1) * 512, :].rearrange("(qq p) c -> p qq c", p=128)
                    for b in range(8)]
        scr1 = w1qt_scr.rearrange("p (ct h) -> p ct h", ct=8)
        w2blocks = [w2d[b * 128:(b + 1) * 128, :] for b in range(8)]

        bcast_row(e, GB, W["L0_ffn_gamma"], ROWF, PSA[:, 0:512], ONES1F, D)
        bcast_row(e, BB0, W["L0_ffn_beta"], ROWF, PSA[:, 0:512], ONES1F, D)
        bcast_row(e, B1B, W["L0_ffn_b1"], ROWF, PSA[:, 0:512], ONES1F, 4096)
        bcast_row(e, B2B, W["L0_ffn_b2"], ROWF, PSA[:, 0:512], ONES1F, D)

        with ExitStack() as st2:
            WQF = st2.enter_context(nc.sbuf_tensor("f0_WQF", [128, 4096], F32))
            WQI = st2.enter_context(nc.sbuf_tensor("f0_WQI", [128, 4096], I32))
            WQB = st2.enter_context(nc.sbuf_tensor("f0_WQB", [128, 4096], BF16))
            W1S = st2.enter_context(nc.sbuf_tensor("f0_W1S", [128, 4096], BF16))

            WQF4 = WQF[:].rearrange("p (qq c) -> p qq c", qq=4)

            def absmean_pass(blocks, col, is4):
                e.inc(v.memset(SC[:, 8:9], 0.0), "V")
                for b in range(8):
                    e.wait(q, "V")
                    dk = e.inc(q.dma_start(out=WQF4 if is4 else WQF[:], in_=blocks[b]), "DH", 16)
                    e.wait(v, "DH", dk)
                    e.vchain(v.reduce_sum(SC[:, 9:10], WQF[:], axis=AX,
                                       apply_absolute_value=True))
                    e.vchain(v.tensor_tensor(SC[:, 8:9], SC[:, 8:9], SC[:, 9:10], ALU.add))
                e.wait(t, "V")
                tk = e.inc(t.matmul(PSC[0:1, 0:1], SC[:, 8:9], ONESCF[:],
                                    start=True, stop=True), "T")
                e.wait(v, "T", tk)
                vv = e.inc(v.tensor_scalar(SM[0:1, 0:1], PSC[0:1, 0:1],
                                           1.0 / (4096.0 * 1024.0), 1e-5, ALU.mult, ALU.max), "V")
                e.wait(t, "V", vv)
                tk = e.inc(t.matmul(PSC[:, 8:9], ONES1F[0:1, :], SM[0:1, 0:1],
                                    start=True, stop=True), "T")
                e.wait(v, "T", tk)
                vv = e.inc(v.tensor_scalar(SWB[:, col + 1:col + 2], PSC[:, 8:9],
                                           1.0 / 127.0, None, ALU.mult), "V")
                e.vchain(v.reciprocal(SWB[:, col:col + 1], PSC[:, 8:9]))
                e.wait(t, "V")

            absmean_pass(w1blocks, 0, True)
            absmean_pass(w2blocks, 2, False)

            def quant_block(wdram_b, swcol, is4):
                e.wait(q, "V")
                dk = e.inc(q.dma_start(out=WQF4 if is4 else WQF[:], in_=wdram_b), "DH", 16)
                e.wait(v, "DH", dk)
                e.inc(v.tensor_scalar(WQI[:], WQF[:], SWB[:, swcol:swcol + 1], None, ALU.mult), "V")
                e.inc(v.tensor_scalar(WQI[:], WQI[:], -1, 1, ALU.max, ALU.min), "V")
                vv = e.inc(v.tensor_copy(WQB[:], WQI[:]), "V")
                e.wait(t, "V", vv)

            for b in range(8):
                quant_block(w1blocks[b], 0, True)
                pairs = []
                for b2 in range(4):
                    for ct in range(8):
                        pairs.append((W1S[:, ct * 512 + b2 * 128: ct * 512 + (b2 + 1) * 128],
                                      WQB[:, b2 * 1024 + ct * 128: b2 * 1024 + (ct + 1) * 128]))
                trans_tiles(e, pairs, PST0, PST1, IDBF)
                sr = W1S[:].rearrange("p (ct x) -> p ct x", ct=8)
                e.wait(g, "V")
                e.inc(g.dma_start(out=scr1[:, :, b * 512:(b + 1) * 512], in_=sr), "DG", 16)
                e.wait(v, "DG")
            scr2 = w2qt_scr.rearrange("p (kt x) -> p kt x", kt=32)
            for b in range(8):
                quant_block(w2blocks[b], 2, False)
                pairs = []
                for kt in range(32):
                    pairs.append((W1S[:, kt * 128:(kt + 1) * 128],
                                  WQB[:, kt * 128:(kt + 1) * 128]))
                trans_tiles(e, pairs, PST0, PST1, IDBF)
                sr = W1S[:].rearrange("p (kt x) -> p kt x", kt=32)
                e.wait(g, "V")
                e.inc(g.dma_start(out=scr2[:, :, b * 128:(b + 1) * 128], in_=sr), "DG", 16)
                e.wait(v, "DG")
        e.barrier()

        W1B0 = sb("f0_W1B0", [128, 8 * D], BF16)
        W1B1 = sb("f0_W1B1", [128, 8 * D], BF16)
        W2B = sb("f0_W2B", [128, 8 * D], BF16)
        XCF = sb("f0_XCF", [128, D], F32)
        XQI = sb("f0_XQI", [128, D], I32)
        XQB = sb("f0_XQB", [128, D], BF16)
        XQT = sb("f0_XQT", [128, D], BF16)
        H1G = sb("f0_H1G", [128, 4096], BF16)
        TF = sb("f0_TF", [128, 512], F32)
        XQ2B = sb("f0_XQ2B", [128, D], BF16)
        XQ2T = sb("f0_XQ2T", [128, 4096], BF16)
        last_w1use = [0, 0]
        last_w2use = 0
        for i in range(NCH):
            xc = X[:, i * D:(i + 1) * D]
            e.vchain(v.tensor_scalar(XCF[:], xc, NEGM[:, i:i + 1], RSTD[:, i:i + 1],
                                     ALU.add, ALU.mult))
            e.vchain(v.reduce_sum(SC[:, 0:1], XCF[:], axis=AX))
            vv = e.inc(v.tensor_scalar(SC[:, 1:2], SC[:, 0:1], -1.0 / D, None, ALU.mult), "V")
            e.wait(s, "V", vv)
            e.inc(s.activation(JUNKF[:], XCF[:], ACT.Square, bias=SC[:, 1:2],
                               scale=1.0, accum_out=SC[:, 2:3]), "S")
            sk = e.inc(s.activation(SC[:, 3:4], SC[:, 2:3], ACT.Sqrt,
                                    bias=EPSC[:, 0:1], scale=1.0 / D), "S")
            e.wait(v, "S", sk)
            e.vchain(v.reciprocal(SC[:, 4:5], SC[:, 3:4]))
            e.inc(v.tensor_scalar(XCF[:], XCF[:], SC[:, 1:2], SC[:, 4:5],
                                  ALU.add, ALU.mult), "V")
            e.inc(v.tensor_tensor(XCF[:], XCF[:], GB[:], ALU.mult), "V")
            e.vchain(v.tensor_tensor(XCF[:], XCF[:], BB0[:], ALU.add))
            e.vchain(v.reduce_max(SC[:, 5:6], XCF[:], axis=AX, apply_absolute_value=True))
            e.vchain(v.tensor_scalar(SC[:, 5:6], SC[:, 5:6], EPSC[:, 0:1], None, ALU.max))
            e.vchain(v.reciprocal(SC[:, 6:7], SC[:, 5:6]))
            e.inc(v.tensor_scalar(SC[:, 6:7], SC[:, 6:7], 127.0, None, ALU.mult), "V")
            e.inc(v.tensor_tensor(SC[:, 7:8], SC[:, 5:6], SWB[:, 1:2], ALU.mult), "V")
            e.inc(v.tensor_scalar(XQI[:], XCF[:], SC[:, 6:7], None, ALU.mult), "V")
            vv = e.inc(v.tensor_copy(XQB[:], XQI[:]), "V")
            e.wait(t, "V", vv)
            pairs = [(XQT[:, c * 128:(c + 1) * 128], XQB[:, c * 128:(c + 1) * 128])
                     for c in range(CT)]
            trans_tiles(e, pairs, PST0, PST1, IDBF)

            for qq in range(4):
                buf = W1B0 if qq % 2 == 0 else W1B1
                if last_w1use[qq % 2]:
                    e.wait(q, "T", last_w1use[qq % 2])
                dk = e.inc(q.dma_start(
                    out=buf[:].rearrange("p (ct x) -> p ct x", ct=8),
                    in_=scr1[:, :, qq * 1024:(qq + 1) * 1024]), "DH", 16)
                for sl in range(2):
                    hof = qq * 1024 + sl * 512
                    bk = sl
                    pb = PSA[:, bk * 512: bk * 512 + 512]
                    e.wait(t, "DH", dk)
                    for ct in range(CT):
                        mm = t.matmul(pb, XQT[:, ct * 128:(ct + 1) * 128],
                                      buf[:, ct * 1024 + sl * 512: ct * 1024 + sl * 512 + 512],
                                      start=(ct == 0), stop=(ct == CT - 1))
                    tk = e.inc(mm, "T")
                    last_w1use[qq % 2] = tk
                    e.wait(v, "T", tk)
                    vv = e.inc(v.scalar_tensor_tensor(TF[:], pb, SC[:, 7:8],
                                                      B1B[:, hof:hof + 512],
                                                      ALU.mult, ALU.add), "V")
                    e.wait(s, "V", vv)
                    sk = e.inc(s.activation(H1G[:, hof:hof + 512], TF[:], ACT.Gelu,
                                            bias=0.0, scale=1.0), "S")
                    e.wait(t, "S", sk)
                    e.wait(v, "S", sk)
            if dbg == "h1" and i == 0:
                e.barrier()
                e.inc(g.dma_start(out=ddbg[:, 0:D], in_=XQB[:]), "DG", 16)
                e.inc(g.dma_start(out=ddbg[:, D:D + 4096], in_=H1G[:]), "DG", 16)
                e.inc(q.dma_start(out=ddbg[:, 5120:5132], in_=SC[:]), "DH", 16)
                e.inc(q.dma_start(out=ddbg[:, 5248:5252], in_=SWB[:]), "DH", 16)
                e.barrier()
                return
            e.vchain(v.reduce_max(SC[:, 5:6], H1G[:], axis=AX, apply_absolute_value=True))
            e.vchain(v.tensor_scalar(SC[:, 5:6], SC[:, 5:6], EPSC[:, 0:1], None, ALU.max))
            e.vchain(v.reciprocal(SC[:, 6:7], SC[:, 5:6]))
            e.inc(v.tensor_scalar(SC[:, 6:7], SC[:, 6:7], 127.0, None, ALU.mult), "V")
            e.inc(v.tensor_tensor(SC[:, 7:8], SC[:, 5:6], SWB[:, 3:4], ALU.mult), "V")
            for pc in range(4):
                e.inc(v.tensor_scalar(XQI[:], H1G[:, pc * D:(pc + 1) * D],
                                      SC[:, 6:7], None, ALU.mult), "V")
                vv = e.inc(v.tensor_copy(XQ2B[:], XQI[:]), "V")
                e.wait(t, "V", vv)
                pairs = [(XQ2T[:, pc * D + c * 128: pc * D + (c + 1) * 128],
                          XQ2B[:, c * 128:(c + 1) * 128]) for c in range(CT)]
                trans_tiles(e, pairs, PST0, PST1, IDBF)
            for qq in range(4):
                if last_w2use:
                    e.wait(q, "T", last_w2use)
                dk = e.inc(q.dma_start(out=W2B[:],
                                       in_=w2qt_scr[:, qq * 8192:(qq + 1) * 8192]), "DH", 16)
                e.wait(t, "DH", dk)
                for nb in range(2):
                    pb = PSC[:, nb * 512: nb * 512 + 512]
                    for kt in range(8):
                        ktg = qq * 8 + kt
                        mm = t.matmul(pb, XQ2T[:, ktg * 128:(ktg + 1) * 128],
                                      W2B[:, kt * 1024 + nb * 512: kt * 1024 + nb * 512 + 512],
                                      start=(qq == 0 and kt == 0), stop=(qq == 3 and kt == 7))
                    tk = e.inc(mm, "T")
                last_w2use = tk
            e.wait(v, "T")
            e.inc(v.scalar_tensor_tensor(xc, PSC[:, 0:1024], SC[:, 7:8],
                                         xc, ALU.mult, ALU.add), "V")
            vv = e.inc(v.tensor_tensor(xc, xc, B2B[:], ALU.add), "V")
            e.wait(t, "V", vv)
    e.barrier()
    if dbg == "ffn":
        with nc.sbuf_tensor("dbgf", [1, 1], F32):
            for c in range(NCH):
                e.inc(q.dma_start(out=ddbg[:, c * D:(c + 1) * D], in_=X[:, c * D:(c + 1) * D]), "DH", 16)
        e.barrier()
        return

    with ExitStack() as st:
        def sb(n, sh, d):
            return st.enter_context(nc.sbuf_tensor(n, sh, d))
        YT = sb("a0_YT", [128, CT * T], BF16)
        QTE = sb("a0_QTE", [128, T], BF16)
        KTE = sb("a0_KTE", [128, T], BF16)
        VTE = sb("a0_VTE", [128, T], BF16)
        KTOK = sb("a0_KTOK", [128, T], BF16)
        VTOK = sb("a0_VTOK", [128, T], BF16)
        WROW = sb("a0_WROW", [128, D], BF16)
        WQT = sb("a0_WQT", [128, D], BF16)
        WKT = sb("a0_WKT", [128, D], BF16)
        WVT = sb("a0_WVT", [128, D], BF16)
        WOT = sb("a0_WOT", [128, 8 * D], BF16)
        BQ = sb("a0_BQ", [128, 8], F32)
        BK = sb("a0_BK", [128, 8], F32)
        BV = sb("a0_BV", [128, 8], F32)
        BOROW = sb("a0_BOROW", [1, D], F32)
        ATM0 = sb("a0_ATM0", [128, 128], BF16)
        ATM1 = sb("a0_ATM1", [128, 128], BF16)
        SSB = sb("a0_SSB", [128, 64], BF16)
        PSA = st.enter_context(nc.psum_tensor("a0_PSA", [128, 1024], F32))
        PSAT0 = st.enter_context(nc.psum_tensor("a0_PSAT0", [128, 128], F32))
        PSAT1 = st.enter_context(nc.psum_tensor("a0_PSAT1", [128, 128], F32))
        PSY = st.enter_context(nc.psum_tensor("a0_PSY", [128, 128], F32))
        PSS = st.enter_context(nc.psum_tensor("a0_PSS", [128, 64], F32))
        PST0 = st.enter_context(nc.psum_tensor("a0_PST0", [128, 128], BF16))
        PST1 = st.enter_context(nc.psum_tensor("a0_PST1", [128, 128], BF16))

        for nm, dst in (("bq", BQ), ("bk", BK), ("bv", BV)):
            e.inc(q.dma_start(out=dst[:],
                              in_=W["L0_attn_" + nm].rearrange("(et p) -> p et", p=128),
                              allow_slow_non_contiguous=True), "DH", 16)
        dk = e.inc(q.dma_start(out=BOROW[0:1, :], in_=W["L0_attn_bo"][None, :]), "DH", 16)
        load_wT(e, W["L0_attn_Wo"], WROW, WOT, PST0, PST1, IDBF, D)
        e.barrier()

        for et in range(8):
            for wnm, wt in (("Wq", WQT), ("Wk", WKT), ("Wv", WVT)):
                wd = W["L0_attn_" + wnm][et * 128:(et + 1) * 128, :]
                e.wait(g, "V")
                dk = e.inc(g.dma_start(out=WROW[:], in_=wd), "DG", 16)
                e.wait(t, "DG", dk)
                pairs = [(wt[:, c * 128:(c + 1) * 128], WROW[:, c * 128:(c + 1) * 128])
                         for c in range(CT)]
                trans_tiles(e, pairs, PST0, PST1, IDBF)
            for wt, bias, dst in ((WQT, BQ, QTE), (WKT, BK, KTE), (WVT, BV, VTE)):
                projT(e, dst, [wt[:, c * 128:(c + 1) * 128] for c in range(CT)],
                      lambda k, off, n: XNT[:, k * T + off: k * T + off + n],
                      PSA, bias_ap=bias[:, et:et + 1])
            pairs = [(KTOK[:, i * 128:(i + 1) * 128], KTE[:, i * 128:(i + 1) * 128])
                     for i in range(NCH)]
            pairs += [(VTOK[:, i * 128:(i + 1) * 128], VTE[:, i * 128:(i + 1) * 128])
                      for i in range(NCH)]
            trans_tiles(e, pairs, PST0, PST1, IDBF)
            e.barrier()

            v_at = [0, 0]
            v_snap = 0
            v_yev = 0
            for i in range(NCH):
                cs = slice(i * 128, (i + 1) * 128)
                for h, (PSAT, ATM) in enumerate(((PSAT0, ATM0), (PSAT1, ATM1))):
                    hr = slice(h * 64, (h + 1) * 64)
                    if v_at[h]:
                        e.wait(t, "V", v_at[h])
                    tk = e.inc(t.matmul(PSAT[:], KTE[hr, cs], QTE[hr, cs],
                                        start=True, stop=True), "T")
                    e.wait(v, "T", tk)
                    v_at[h] = e.inc(v.tensor_tensor(ATM[:], PSAT[:], MASKU[:], ALU.mult), "V")
                if v_yev:
                    e.wait(t, "V", v_yev)
                e.wait(t, "V", v_at[1])
                for h in range(2):
                    hr = slice(h * 64, (h + 1) * 64)
                    ATM = ATM0 if h == 0 else ATM1
                    po = PSY[0:64, :] if h == 0 else PSY[64:128, :]
                    mm = t.matmul(po, VTOK[:, i * 128 + h * 64: i * 128 + h * 64 + 64],
                                  ATM[:], start=True, stop=(i == 0),
                                  tile_position=None if h == 0 else (0, 64))
                    if i > 0:
                        mm = t.matmul(po, SSB[hr, :], QTE[hr, cs],
                                      start=False, stop=True,
                                      tile_position=None if h == 0 else (64, 64))
                    e.inc(mm, "T")
                if v_snap:
                    e.wait(t, "V", v_snap)
                for h in range(2):
                    po = PSS[0:64, :] if h == 0 else PSS[64:128, :]
                    mm = t.matmul(po, KTOK[:, i * 128 + h * 64: i * 128 + h * 64 + 64],
                                  VTOK[:, i * 128 + h * 64: i * 128 + h * 64 + 64],
                                  start=(i == 0), stop=(i == NCH - 1),
                                  tile_position=None if h == 0 else (0, 64))
                tk_s = e.inc(mm, "T")
                e.wait(v, "T", tk_s)
                v_snap = e.inc(v.tensor_copy(SSB[:], PSS[:]), "V")
                v_yev = e.inc(v.tensor_copy(YT[:, et * T + i * 128: et * T + (i + 1) * 128],
                                            PSY[:]), "V")
            e.barrier()

        for i in range(NCH):
            for nb in range(2):
                pb = PSA[:, nb * 512: nb * 512 + 512]
                mm = t.matmul(pb, ONES1F[0:1, :], BOROW[0:1, nb * 512:(nb + 1) * 512],
                              start=True, stop=False)
                for et in range(8):
                    mm = t.matmul(pb, YT[:, et * T + i * 128: et * T + (i + 1) * 128],
                                  WOT[:, et * D + nb * 512: et * D + nb * 512 + 512],
                                  start=False, stop=(et == 7))
                tk = e.inc(mm, "T")
            e.wait(v, "T", tk)
            vv = e.inc(v.tensor_tensor(X[:, i * D:(i + 1) * D], X[:, i * D:(i + 1) * D],
                                       PSA[:, 0:1024], ALU.add), "V")
            e.wait(t, "V", vv)
    e.barrier()


def emit_layer1(e, X, XNT, NEGM, RSTD, SS, JUNKF, XNC, IDBF, IDF, MASKU, MASKUB,
                c_sel16, ONES1F, ONESCB, ONESCF, EPSC, EPS9, ROWF, W, w1qt_scr, w2qt_scr,
                dbg=None, ddbg=None):
    nc = e.nc
    t, v, s, q, g = e.t, e.v, e.s, e.q, e.g

    emit_ln(e, X, NEGM, RSTD, JUNKF, SS, EPSC)
    e.barrier()
    with ExitStack() as st:
        PST0 = st.enter_context(nc.psum_tensor("l1_PST0", [128, 128], BF16))
        PST1 = st.enter_context(nc.psum_tensor("l1_PST1", [128, 128], BF16))
        emit_xnt(e, X, XNT, NEGM, RSTD, XNC, PST0, PST1, IDBF)
    e.barrier()

    with ExitStack() as st:
        def sb(n, sh, d):
            return st.enter_context(nc.sbuf_tensor(n, sh, d))
        WB1 = sb("f1_WB1", [128, 8 * D], BF16)
        WB2 = sb("f1_WB2", [128, 8 * D], BF16)
        GELH = sb("f1_GELH", [128, 8 * T], BF16)
        WGB = sb("f1_WGB", [128, 64], BF16)
        BGB = sb("f1_BGB", [128, 8], F32)
        LG = sb("f1_LG", [128, 8], F32)
        LMP = sb("f1_LMP", [128, 2], F32)
        WMASK = sb("f1_WMASK", [128, NCH * 8], F32)
        WMB = sb("f1_WMB", [128, 8], BF16)
        MASKWT = sb("f1_MASKWT", [8, NCH * 128], BF16)
        B1E = sb("f1_B1E", [128, 8], F32)
        B2SB = sb("f1_B2SB", [8, D], BF16)
        w1flat = W["L1_ffn_W1"].rearrange("ee r c -> (ee r) c")
        w2flat = W["L1_ffn_W2"].rearrange("ee r c -> (ee r) c")
        b1flat = W["L1_ffn_b1"].rearrange("ee dd -> (ee dd)")

        dk = e.inc(g.dma_start(out=WGB[:].rearrange("p (ct j) -> p ct j", ct=8),
                               in_=W["L1_ffn_Wg"].rearrange("(ct p) j -> p ct j", p=128),
                               allow_slow_non_contiguous=True), "DG", 16)
        e.inc(g.dma_start(out=B2SB[:], in_=W["L1_ffn_b2"]), "DG", 16)
        LGD = sb("f1_LGD", [128, 128], F32)
        SSD = sb("f1_SSD", [128, 64], F32)
        with ExitStack() as st2:
            PSR = st2.enter_context(nc.psum_tensor("f1_PSR", [128, 512], F32))
            PSTB = st2.enter_context(nc.psum_tensor("f1_PSTB", [128, 128], BF16))
            bcast_row(e, BGB, W["L1_ffn_bg"], ROWF, PSR, ONES1F, 8)
            e.barrier()
            for i in range(NCH):
                for ct in range(CT):
                    mm = t.matmul(PSR[:, 0:8], XNT[:, ct * T + i * 128: ct * T + (i + 1) * 128],
                                  WGB[:, ct * 8:(ct + 1) * 8], start=(ct == 0), stop=(ct == CT - 1))
                tk = e.inc(mm, "T")
                e.wait(v, "T", tk)
                e.wait(v, "S")
                e.vchain(v.tensor_tensor(LG[:], PSR[:, 0:8], BGB[:], ALU.add))
                e.vchain(v.reduce_max(LMP[:, 0:1], LG[:], axis=AX))
                vv = e.inc(v.tensor_scalar(LMP[:, 1:2], LMP[:, 0:1], -1.0, None, ALU.mult), "V")
                e.wait(s, "V", vv)
                sk = e.inc(s.activation(JUNKF[:, 0:8], LG[:], ACT.Exp,
                                        bias=LMP[:, 1:2], scale=1.0, accum_out=SS[:, 0:1]), "S")
                e.wait(v, "S", sk)
                e.inc(v.tensor_copy(LGD[:, i * 8:(i + 1) * 8], LG[:]), "V")
                e.inc(v.tensor_copy(SSD[:, 4 * i:4 * i + 2], LMP[:, 0:2]), "V")
                e.inc(v.tensor_copy(SSD[:, 4 * i + 2:4 * i + 3], SS[:, 0:1]), "V")
                e.vchain(v.reciprocal(SS[:, 1:2], SS[:, 0:1]))
                e.inc(v.tensor_copy(SSD[:, 4 * i + 3:4 * i + 4], SS[:, 1:2]), "V")
                e.inc(v.tensor_scalar(LG[:], LG[:], LMP[:, 0:1], None, ALU.is_equal), "V")
                vv = e.inc(v.tensor_scalar(WMASK[:, i * 8:(i + 1) * 8], LG[:],
                                           SS[:, 1:2], None, ALU.mult), "V")
                e.inc(v.tensor_copy(WMB[:], WMASK[:, i * 8:(i + 1) * 8]), "V")
                e.wait(t, "V")
                tk = e.inc(t.transpose(PSTB[0:8, 0:128], WMB[:], IDBF[:]), "T")
                e.wait(v, "T", tk)
                vv = e.inc(v.tensor_copy(MASKWT[0:8, i * 128:(i + 1) * 128],
                                         PSTB[0:8, 0:128]), "V")
                e.wait(t, "V", vv)
        e.barrier()
        if dbg == "l1r":
            e.inc(q.dma_start(out=ddbg[:, 0:NCH * 8], in_=WMASK[:]), "DH", 16)
            e.inc(q.dma_start(out=ddbg[:, 256:384], in_=LGD[:]), "DH", 16)
            e.inc(q.dma_start(out=ddbg[:, 384:448], in_=SSD[:]), "DH", 16)
            e.barrier()
            return

        with ExitStack() as st2:
            PSA = st2.enter_context(nc.psum_tensor("f1_PSA", [128, 2048], F32))
            PSB = st2.enter_context(nc.psum_tensor("f1_PSB", [128, 2048], F32))
            for ex in range(8):
                e.wait(g, "T")
                e.wait(g, "S")
                dk = e.inc(g.dma_start(out=WB1[:].rearrange("p (ct c) -> p ct c", ct=8),
                                       in_=w1flat[ex * D:(ex + 1) * D, :].rearrange(
                                           "(ct p) c -> p ct c", p=128)), "DG", 16)
                dk2 = e.inc(g.dma_start(out=WB2[:].rearrange("p (kt c) -> p kt c", kt=8),
                                        in_=w2flat[ex * D:(ex + 1) * D, :].rearrange(
                                            "(kt p) c -> p kt c", p=128)), "DG", 16)
                dk3 = e.inc(g.dma_start(out=B1E[:], in_=b1flat[ex * D:(ex + 1) * D].rearrange(
                    "(ct p) -> p ct", p=128), allow_slow_non_contiguous=True), "DG", 16)
                e.wait(t, "DG", dk)
                e.wait(s, "DG", dk3)
                last_ev = [0, 0]
                for ht in range(8):
                    PS = PSA if ht % 2 == 0 else PSB
                    if last_ev[ht % 2]:
                        e.wait(t, "S", last_ev[ht % 2])
                    for nc4 in range(4):
                        for ct in range(CT):
                            mm = t.matmul(PS[:, nc4 * 512:(nc4 + 1) * 512],
                                          WB1[:, ct * D + ht * 128: ct * D + (ht + 1) * 128],
                                          XNT[:, ct * T + nc4 * 512: ct * T + (nc4 + 1) * 512],
                                          start=(ct == 0), stop=(ct == CT - 1))
                    tk = e.inc(mm, "T")
                    e.wait(s, "T", tk)
                    last_ev[ht % 2] = e.inc(s.activation(GELH[:, ht * T:(ht + 1) * T], PS[:, 0:2048],
                                                         ACT.Gelu, bias=B1E[:, ht:ht + 1],
                                                         scale=1.0), "S")
                e.wait(t, "S")
                for i in range(NCH):
                    pb2 = PSA if i % 2 == 0 else PSB
                    for nb in range(2):
                        for kt in range(8):
                            mm = t.matmul(pb2[:, nb * 512:(nb + 1) * 512],
                                          GELH[:, kt * T + i * 128: kt * T + (i + 1) * 128],
                                          WB2[:, kt * D + nb * 512: kt * D + nb * 512 + 512],
                                          start=(kt == 0), stop=(kt == 7))
                    tk = e.inc(mm, "T")
                    e.wait(v, "T", tk)
                    vv = e.inc(v.scalar_tensor_tensor(X[:, i * D:(i + 1) * D], pb2[:, 0:1024],
                                                      WMASK[:, i * 8 + ex: i * 8 + ex + 1],
                                                      X[:, i * D:(i + 1) * D],
                                                      ALU.mult, ALU.add), "V")
                    if i >= 2:
                        pass
                    e.wait(t, "V", vv)
            for i in range(NCH):
                for nb in range(2):
                    mm = t.matmul(PSA[:, nb * 512:(nb + 1) * 512],
                                  MASKWT[0:8, i * 128:(i + 1) * 128],
                                  B2SB[0:8, nb * 512:(nb + 1) * 512],
                                  start=True, stop=True)
                tk = e.inc(mm, "T")
                e.wait(v, "T", tk)
                vv = e.inc(v.tensor_tensor(X[:, i * D:(i + 1) * D], X[:, i * D:(i + 1) * D],
                                           PSA[:, 0:1024], ALU.add), "V")
                e.wait(t, "V", vv)
    e.barrier()
    if dbg == "l1f":
        for c in range(NCH):
            e.inc(q.dma_start(out=ddbg[:, c * D:(c + 1) * D], in_=X[:, c * D:(c + 1) * D]), "DH", 16)
        e.barrier()
        return

    with ExitStack() as st:
        def sb(n, sh, d):
            return st.enter_context(nc.sbuf_tensor(n, sh, d))
        KALLT = XNT
        VAH = sb("a1_VAH", [128, NCH * 1056], BF16)
        Q1T = sb("a1_Q1T", [128, 4 * T], BF16)
        RZ = sb("a1_RZ", [128, 1], F32)

        with ExitStack() as stm:
            def sbm(n, sh, d):
                return stm.enter_context(nc.sbuf_tensor(n, sh, d))
            M1T = sbm("a1_M1T", [128, 4 * D], BF16)
            M2 = sbm("a1_M2", [128, 4 * D], BF16)
            PSA = stm.enter_context(nc.psum_tensor("a1_PSA", [128, 1024], F32))
            PST0 = stm.enter_context(nc.psum_tensor("a1_PST0", [128, 128], BF16))
            PST1 = stm.enter_context(nc.psum_tensor("a1_PST1", [128, 128], BF16))

            with ExitStack() as stt1:
                T1T = stt1.enter_context(nc.sbuf_tensor("a1_T1T", [128, 4 * 512], BF16))
                with ExitStack() as st2:
                    UQB = st2.enter_context(nc.sbuf_tensor("a1_UQB", [128, 8 * 512], BF16))
                    UKB = st2.enter_context(nc.sbuf_tensor("a1_UKB", [128, 8 * 512], BF16))
                    for wnm, dst in (("W_uq", UQB), ("W_uk", UKB)):
                        e.wait(g, "T")
                        e.inc(g.dma_start(out=dst[:].rearrange("p (ct c) -> p ct c", ct=8),
                                          in_=W["L1_attn_" + wnm].rearrange("(ct p) c -> p ct c", p=128)),
                              "DG", 16)
                    e.barrier()
                    for mt in range(4):
                        pb = PSA[:, (mt % 2) * 512:(mt % 2) * 512 + 512]
                        e.wait(t, "V")
                        for ct in range(CT):
                            mm = t.matmul(pb, UQB[:, ct * 512 + mt * 128: ct * 512 + (mt + 1) * 128],
                                          UKB[:, ct * 512:(ct + 1) * 512],
                                          start=(ct == 0), stop=(ct == CT - 1))
                        tk = e.inc(mm, "T")
                        e.wait(v, "T", tk)
                        vv = e.inc(v.tensor_copy(T1T[:, mt * 512:(mt + 1) * 512], pb), "V")
                    e.wait(t, "V", vv)
                    e.barrier()
                with ExitStack() as st2:
                    WDQS = st2.enter_context(nc.sbuf_tensor("a1_WDQS", [128, 4 * D], BF16))
                    e.inc(g.dma_start(out=WDQS[:].rearrange("p (kt c) -> p kt c", kt=4),
                                      in_=W["L1_attn_W_dq"].rearrange("(kt p) c -> p kt c", p=128)),
                          "DG", 16)
                    e.barrier()
                    for mt in range(4):
                        for nb in range(2):
                            pb = PSA[:, nb * 512: nb * 512 + 512]
                            for kt in range(4):
                                mm = t.matmul(pb, T1T[:, kt * 512 + mt * 128: kt * 512 + (mt + 1) * 128],
                                              WDQS[:, kt * D + nb * 512: kt * D + nb * 512 + 512],
                                              start=(kt == 0), stop=(kt == 3))
                            e.inc(mm, "T")
                        e.wait(v, "T")
                        vv = e.inc(v.tensor_copy(M1T[:, mt * D:(mt + 1) * D], PSA[:, 0:1024]), "V")
                        e.wait(t, "V", vv)
                    e.barrier()
            with ExitStack() as st2:
                UVB = st2.enter_context(nc.sbuf_tensor("a1_UVB", [128, 8 * 512], BF16))
                WOT = st2.enter_context(nc.sbuf_tensor("a1_WOT", [128, 8 * D], BF16))
                WROW = st2.enter_context(nc.sbuf_tensor("a1_WROW", [128, D], BF16))
                e.wait(g, "T")
                e.inc(g.dma_start(out=UVB[:].rearrange("p (ct c) -> p ct c", ct=8),
                                  in_=W["L1_attn_W_uv"].rearrange("(ct p) c -> p ct c", p=128)),
                      "DG", 16)
                load_wT(e, W["L1_attn_W_o"], WROW, WOT, PST0, PST1, IDBF, D)
                e.barrier()
                for mt in range(4):
                    for nb in range(2):
                        pb = PSA[:, nb * 512: nb * 512 + 512]
                        for ct in range(CT):
                            mm = t.matmul(pb, UVB[:, ct * 512 + mt * 128: ct * 512 + (mt + 1) * 128],
                                          WOT[:, ct * D + nb * 512: ct * D + nb * 512 + 512],
                                          start=(ct == 0), stop=(ct == CT - 1))
                        e.inc(mm, "T")
                    e.wait(v, "T")
                    vv = e.inc(v.tensor_copy(M2[:, mt * D:(mt + 1) * D], PSA[:, 0:1024]), "V")
                    e.wait(t, "V", vv)
            CKVT = sbm("a1_CKVT", [128, 4 * T], BF16)
            with ExitStack() as st2:
                WDT = st2.enter_context(nc.sbuf_tensor("a1_WDT", [128, 8 * 512], BF16))
                WROW = st2.enter_context(nc.sbuf_tensor("a1_WROW2", [128, D], BF16))
                for wnm, dst in (("W_dkv", CKVT), ("W_dq", Q1T)):
                    wd = W["L1_attn_" + wnm]
                    for b in range(4):
                        e.wait(g, "V")
                        dk = e.inc(g.dma_start(out=WROW[:], in_=wd[b * 128:(b + 1) * 128, :]), "DG", 16)
                        e.wait(t, "DG", dk)
                        pairs = [(WDT[:, c * 512 + b * 128: c * 512 + (b + 1) * 128],
                                  WROW[:, c * 128:(c + 1) * 128]) for c in range(CT)]
                        trans_tiles(e, pairs, PST0, PST1, IDBF)
                    for mt in range(4):
                        projT(e, dst[:, mt * T:(mt + 1) * T],
                              [WDT[:, c * 512 + mt * 128: c * 512 + (mt + 1) * 128] for c in range(CT)],
                              lambda k, off, n: XNT[:, k * T + off: k * T + off + n], PSA)
            e.barrier()
            for mt in range(8):
                projT(e, KALLT[:, mt * T:(mt + 1) * T],
                      [M1T[:, k * D + mt * 128: k * D + (mt + 1) * 128] for k in range(4)],
                      lambda k, off, n: CKVT[:, k * T + off: k * T + off + n], PSA)
            e.inc(v.memset(VAH[:].rearrange("p (a c) -> p a c", c=66)[:, :, 64:65], 1.0), "V")
            e.inc(v.memset(VAH[:].rearrange("p (a c) -> p a c", c=66)[:, :, 65:66], 0.0), "V")
            for i in range(NCH):
                for nb in range(2):
                    pb = PSA[:, nb * 512: nb * 512 + 512]
                    for kt in range(4):
                        mm = t.matmul(pb, CKVT[:, kt * T + i * 128: kt * T + (i + 1) * 128],
                                      M2[:, kt * D + nb * 512: kt * D + nb * 512 + 512],
                                      start=(kt == 0), stop=(kt == 3))
                    e.inc(mm, "T")
                e.wait(v, "T")
                vv = e.inc(v.tensor_copy(
                    VAH[:, i * 1056:(i + 1) * 1056].rearrange("p (hh c) -> p hh c", c=66)[:, :, 0:64],
                    PSA[:, 0:1024].rearrange("p (hh c) -> p hh c", c=64)), "V")
                e.wait(t, "V", vv)
        e.barrier()

        with ExitStack() as st2:
            def sb2(n, sh, d):
                return st2.enter_context(nc.sbuf_tensor(n, sh, d))
            WUQT = sb2("a1_WUQT", [128, 4 * D], BF16)
            Q2TE = sb2("a1_Q2TE", [128, T], BF16)
            UJ0 = sb2("a1_UJ0", [128, T], BF16)
            UJ1 = sb2("a1_UJ1", [128, T], BF16)
            WROW = sb2("a1_WROW3", [128, 512], BF16)
            PU0 = st2.enter_context(nc.psum_tensor("a1_PU0", [128, 512], F32))
            PU1 = st2.enter_context(nc.psum_tensor("a1_PU1", [128, 512], F32))
            PQ = st2.enter_context(nc.psum_tensor("a1_PQ", [128, 512], F32))
            PST0 = st2.enter_context(nc.psum_tensor("a1_PST0b", [128, 128], BF16))
            PST1 = st2.enter_context(nc.psum_tensor("a1_PST1b", [128, 128], BF16))
            PAV = [st2.enter_context(nc.psum_tensor(f"a1_PAV{b}", [128, 512], F32))
                   for b in range(3)]
            for b in range(8):
                e.wait(g, "V")
                dk = e.inc(g.dma_start(out=WROW[:, 0:512],
                                       in_=W["L1_attn_W_uq"][b * 128:(b + 1) * 128, :]), "DG", 16)
                e.wait(t, "DG", dk)
                pairs = [(WUQT[:, c * D + b * 128: c * D + (b + 1) * 128],
                          WROW[:, c * 128:(c + 1) * 128]) for c in range(4)]
                trans_tiles(e, pairs, PST0, PST1, IDBF)
            e.barrier()
            for et in range(8):
                for nc4 in range(4):
                    for kt in range(4):
                        mm = t.matmul(PQ[:, 0:512],
                                      WUQT[:, kt * D + et * 128: kt * D + (et + 1) * 128],
                                      Q1T[:, kt * T + nc4 * 512: kt * T + (nc4 + 1) * 512],
                                      start=(kt == 0), stop=(kt == 3))
                    tk = e.inc(mm, "T")
                    e.wait(v, "T", tk)
                    vv = e.inc(v.tensor_copy(Q2TE[:, nc4 * 512:(nc4 + 1) * 512], PQ[:, 0:512]), "V")
                    e.wait(t, "V", vv)
                for hh in range(2):
                    h = 2 * et + hh
                    hr = slice(hh * 64, (hh + 1) * 64)
                    last_s = [0, 0]
                    last_av = 0
                    for j in range(NCH):
                        UJ = UJ0 if j % 2 == 0 else UJ1
                        if last_av:
                            e.wait(s, "T", last_av)
                        for sc in range(j // 4, 4):
                            pu = PU0 if sc % 2 == 0 else PU1
                            if last_s[sc % 2]:
                                e.wait(t, "S", last_s[sc % 2])
                            tk = e.inc(t.matmul(pu[:],
                                                KALLT[hr, et * T + j * 128: et * T + (j + 1) * 128],
                                                Q2TE[hr, sc * 512:(sc + 1) * 512],
                                                start=True, stop=True), "T")
                            e.wait(s, "T", tk)
                            last_s[sc % 2] = e.inc(s.activation(UJ[:, sc * 512:(sc + 1) * 512],
                                                                pu[:], ACT.Exp,
                                                                bias=0.0, scale=0.125), "S")
                        e.wait(v, "S")
                        vv = e.inc(v.tensor_tensor(UJ[:, j * 128:(j + 1) * 128],
                                                   UJ[:, j * 128:(j + 1) * 128],
                                                   MASKUB[:], ALU.mult), "V")
                        e.wait(t, "V", vv)
                        e.wait(t, "S")
                        for qc in range(j, NCH):
                            bk, sl = (2, qc - 14) if qc >= 14 else (qc // 7, qc % 7)
                            mm = t.matmul(PAV[bk][:, sl * 68: sl * 68 + 66],
                                          UJ[:, qc * 128:(qc + 1) * 128],
                                          VAH[:, j * 1056 + h * 66: j * 1056 + (h + 1) * 66],
                                          start=(j == 0 and qc in (0, 7, 14)), stop=(j == qc))
                        last_av = e.inc(mm, "T")
                        ready = []
                        if j == 6:
                            ready = [(0, qc2) for qc2 in range(0, 7)]
                        elif j == 13:
                            ready = [(1, qc2) for qc2 in range(7, 14)]
                        elif j == 15:
                            ready = [(2, qc2) for qc2 in range(14, 16)]
                        if ready:
                            e.wait(v, "T", last_av)
                            for bk, qc2 in ready:
                                sl = qc2 - (0 if bk == 0 else 7 if bk == 1 else 14)
                                e.vchain(v.reciprocal(RZ[:], PAV[bk][:, sl * 68 + 64: sl * 68 + 65]))
                                vv = e.inc(v.scalar_tensor_tensor(
                                    X[:, qc2 * D + h * 64: qc2 * D + (h + 1) * 64],
                                    PAV[bk][:, sl * 68: sl * 68 + 64], RZ[:, 0:1],
                                    X[:, qc2 * D + h * 64: qc2 * D + (h + 1) * 64],
                                    ALU.mult, ALU.add), "V")
                            e.wait(t, "V", vv)
    e.barrier()


def emit_layer2(e, X, XNT, NEGM, RSTD, SS, JUNKF, XNC, IDBF, IDF, MASKU, MASKUB,
                c_sel16, ONES1F, ONESCB, ONESCF, EPSC, EPS9, ROWF, W, w1qt_scr, w2qt_scr,
                dbg=None, ddbg=None):
    nc = e.nc
    t, v, s, q, g = e.t, e.v, e.s, e.q, e.g

    emit_ln(e, X, NEGM, RSTD, JUNKF, SS, EPSC)
    e.barrier()
    with ExitStack() as st:
        PST0 = st.enter_context(nc.psum_tensor("l2_PST0", [128, 128], BF16))
        PST1 = st.enter_context(nc.psum_tensor("l2_PST1", [128, 128], BF16))
        emit_xnt(e, X, XNT, NEGM, RSTD, XNC, PST0, PST1, IDBF)
    e.barrier()

    with ExitStack() as st:
        def sb(n, sh, d):
            return st.enter_context(nc.sbuf_tensor(n, sh, d))
        XN2T = sb("a2_XN2T", [128, CT * T], BF16)
        QT2 = sb("a2_QT2", [128, T], BF16)
        KT2 = sb("a2_KT2", [128, T], BF16)
        VT2 = sb("a2_VT2", [128, T], BF16)
        KSM = sb("a2_KSM", [128, T], BF16)
        KTOK = sb("a2_KTOK", [128, T], BF16)
        VTOK = sb("a2_VTOK", [128, T], BF16)
        OUTT = sb("a2_OUTT", [128, T], BF16)
        CTX = sb("a2_CTX", [128, 32], BF16)
        WQKV = sb("a2_WQKV", [128, 8 * 384], BF16)
        WOUT2 = sb("a2_WOUT2", [128, D], BF16)
        GB2 = sb("a2_GB2", [128, D], F32)
        BB2 = sb("a2_BB2", [128, D], F32)
        RP = sb("a2_RP", [128, NCH], F32)
        NMR = sb("a2_NMR", [128, NCH], F32)
        XCF = sb("a2_XCF", [128, D], F32)
        SC = sb("a2_SC", [128, 8], F32)
        PSA = st.enter_context(nc.psum_tensor("a2_PSA", [128, 1024], F32))
        PCX = st.enter_context(nc.psum_tensor("a2_PCX", [128, 64], F32))
        PST0 = st.enter_context(nc.psum_tensor("a2_PST0", [128, 128], BF16))
        PST1 = st.enter_context(nc.psum_tensor("a2_PST1", [128, 128], BF16))

        e.inc(g.dma_start(out=WQKV[:].rearrange("p (ct k) -> p ct k", ct=8),
                          in_=W["L2_attn_w_qkv"].rearrange("(ct p) k -> p ct k", p=128)), "DG", 16)
        e.inc(g.dma_start(out=WOUT2[:], in_=W["L2_attn_w_out"]), "DG", 16)
        bcast_row(e, GB2, W["L2_attn_gamma"].rearrange("a b d -> (a b d)"), ROWF, PSA[:, 0:512], ONES1F, D)
        bcast_row(e, BB2, W["L2_attn_beta"].rearrange("a b d -> (a b d)"), ROWF, PSA[:, 0:512], ONES1F, D)
        e.inc(v.tensor_scalar(RP[:], RSTD[:], 1.0, None, ALU.add), "V")
        vv = e.inc(v.tensor_tensor(NMR[:], NEGM[:], RSTD[:], ALU.mult), "V")
        e.wait(t, "V", vv)
        e.barrier()

        for i in range(NCH):
            e.wait(v, "S")
            e.vchain(v.tensor_scalar(XCF[:], X[:, i * D:(i + 1) * D], NEGM[:, i:i + 1],
                                     RSTD[:, i:i + 1], ALU.add, ALU.mult))
            e.vchain(v.reduce_sum(SC[:, 0:1], XCF[:], axis=AX))
            vv = e.inc(v.tensor_scalar(SC[:, 1:2], SC[:, 0:1], -1.0 / D, None, ALU.mult), "V")
            e.wait(s, "V", vv)
            sk0 = e.inc(s.activation(JUNKF[:, 0:D], XCF[:], ACT.Square, bias=SC[:, 1:2],
                                     scale=1.0, accum_out=SC[:, 2:3]), "S")
            sk = e.inc(s.activation(SC[:, 3:4], SC[:, 2:3], ACT.Sqrt,
                                    bias=EPSC[:, 0:1], scale=1.0 / D), "S")
            e.wait(v, "S", sk)
            e.vchain(v.reciprocal(SC[:, 4:5], SC[:, 3:4]))
            e.inc(v.tensor_scalar(XCF[:], XCF[:], SC[:, 1:2], SC[:, 4:5], ALU.add, ALU.mult), "V")
            e.inc(v.tensor_tensor(XCF[:], XCF[:], GB2[:], ALU.mult), "V")
            e.inc(v.tensor_tensor(XCF[:], XCF[:], BB2[:], ALU.add), "V")
            vv = e.inc(v.tensor_copy(XNC[:], XCF[:]), "V")
            e.wait(t, "V", vv)
            pairs = [(XN2T[:, c * T + i * 128: c * T + (i + 1) * 128],
                      XNC[:, c * 128:(c + 1) * 128]) for c in range(CT)]
            trans_tiles(e, pairs, PST0, PST1, IDBF)
        for part, dst in ((0, QT2), (1, KT2), (2, VT2)):
            projT(e, dst, [WQKV[:, c * 384 + part * 128: c * 384 + (part + 1) * 128]
                           for c in range(CT)],
                  lambda k, off, n: XN2T[:, k * T + off: k * T + off + n], PSA)
        e.vchain(v.reduce_max(SC[:, 0:1], KT2[:], axis=AX, negate=True))
        e.wait(s, "V")
        sk = e.inc(s.activation(KSM[:], KT2[:], ACT.Exp, bias=SC[:, 0:1], scale=1.0,
                                accum_out=SC[:, 1:2]), "S")
        e.wait(v, "S", sk)
        e.vchain(v.reciprocal(SC[:, 2:3], SC[:, 1:2]))
        vv = e.inc(v.tensor_scalar(KSM[:], KSM[:], SC[:, 2:3], None, ALU.mult), "V")
        e.wait(t, "V", vv)
        pairs = [(KTOK[:, i * 128:(i + 1) * 128], KSM[:, i * 128:(i + 1) * 128])
                 for i in range(NCH)]
        pairs += [(VTOK[:, i * 128:(i + 1) * 128], VT2[:, i * 128:(i + 1) * 128])
                  for i in range(NCH)]
        trans_tiles(e, pairs, PST0, PST1, IDBF)
        e.barrier()
        for h in range(4):
            for i in range(NCH):
                mm = t.matmul(PCX[32 * h:32 * h + 32, 0:32],
                              KTOK[:, i * 128 + h * 32: i * 128 + h * 32 + 32],
                              VTOK[:, i * 128 + h * 32: i * 128 + h * 32 + 32],
                              start=(i == 0), stop=(i == NCH - 1),
                              tile_position=None if h == 0 else (0, 32 * h))
            e.inc(mm, "T")
        e.wait(v, "T")
        vv = e.inc(v.tensor_scalar(CTX[:], PCX[:, 0:32], 32.0 ** -0.5, None, ALU.mult), "V")
        e.wait(t, "V", vv)
        for nc4 in range(4):
            for h in range(4):
                mm = t.matmul(PSA[32 * h:32 * h + 32, 0:512],
                              CTX[32 * h:32 * h + 32, 0:32],
                              QT2[32 * h:32 * h + 32, nc4 * 512:(nc4 + 1) * 512],
                              start=True, stop=True,
                              tile_position=(32 * h, 32 * h) if h else None)
            tk = e.inc(mm, "T")
            e.wait(v, "T", tk)
            vv = e.inc(v.tensor_copy(OUTT[:, nc4 * 512:(nc4 + 1) * 512], PSA[:, 0:512]), "V")
            e.wait(t, "V", vv)
        e.wait(q, "S")
        dk = e.inc(q.dma_start(out=JUNKF[0:1, 0:D], in_=W["L2_attn_b_out"][None, :]), "DH", 16)
        e.wait(t, "DH", dk)
        for i in range(NCH):
            for nb in range(2):
                pb = PSA[:, nb * 512: nb * 512 + 512]
                mm = t.matmul(pb, ONES1F[0:1, :], JUNKF[0:1, nb * 512:(nb + 1) * 512],
                              start=True, stop=False)
                mm = t.matmul(pb, OUTT[:, i * 128:(i + 1) * 128],
                              WOUT2[:, nb * 512:(nb + 1) * 512],
                              start=False, stop=True)
            tk = e.inc(mm, "T")
            e.wait(v, "T", tk)
            xc = X[:, i * D:(i + 1) * D]
            e.inc(v.scalar_tensor_tensor(xc, xc, RP[:, i:i + 1], PSA[:, 0:1024],
                                         ALU.mult, ALU.add), "V")
            vv = e.inc(v.tensor_scalar(xc, xc, NMR[:, i:i + 1], None, ALU.add), "V")
            e.wait(t, "V", vv)
    e.barrier()
    if dbg == "l2a":
        for c in range(NCH):
            e.inc(q.dma_start(out=ddbg[:, c * D:(c + 1) * D], in_=X[:, c * D:(c + 1) * D]), "DH", 16)
        e.barrier()
        return

    with ExitStack() as st:
        def sb(n, sh, d):
            return st.enter_context(nc.sbuf_tensor(n, sh, d))
        H1T = sb("f2_H1T", [128, CT * T], BF16)
        WA = sb("f2_WA", [128, 8 * D], BF16)
        TMP = sb("f2_TMP", [128, D], BF16)
        PSA = st.enter_context(nc.psum_tensor("f2_PSA", [128, 1024], F32))

        def loadw(name):
            e.wait(g, "T")
            dk = e.inc(g.dma_start(out=WA[:].rearrange("p (ct c) -> p ct c", ct=8),
                                   in_=W[name].rearrange("(ct p) c -> p ct c", p=128)), "DG", 16)
            e.wait(t, "DG", dk)

        loadw("L2_ffn_weight")
        for mt in range(8):
            projT(e, H1T[:, mt * T:(mt + 1) * T],
                  [WA[:, c * D + mt * 128: c * D + (mt + 1) * 128] for c in range(CT)],
                  lambda k, off, n: XNT[:, k * T + off: k * T + off + n], PSA)
        H2T = sb("f2_H2T", [128, CT * T], BF16)
        loadw("L2_ffn_aux_weight1")
        for mt in range(8):
            projT(e, H2T[:, mt * T:(mt + 1) * T],
                  [WA[:, c * D + mt * 128: c * D + (mt + 1) * 128] for c in range(CT)],
                  lambda k, off, n: H1T[:, k * T + off: k * T + off + n], PSA)
        loadw("L2_ffn_aux_weight2")
        dk = e.inc(q.dma_start(out=JUNKF[0:1, 0:D], in_=W["L2_ffn_bias"][None, :]), "DH", 16)
        e.wait(t, "DH", dk)
        for i in range(NCH):
            e.wait(t, "S")
            for nb in range(2):
                pb = PSA[:, nb * 512: nb * 512 + 512]
                mm = t.matmul(pb, ONES1F[0:1, :], JUNKF[0:1, nb * 512:(nb + 1) * 512],
                              start=True, stop=False)
                for kt in range(8):
                    mm = t.matmul(pb, H2T[:, kt * T + i * 128: kt * T + (i + 1) * 128],
                                  WA[:, kt * D + nb * 512: kt * D + nb * 512 + 512],
                                  start=False, stop=(kt == 7))
                tk = e.inc(mm, "T")
            e.wait(s, "T", tk)
            sk = e.inc(s.activation(TMP[:], PSA[:, 0:1024], ACT.Tanh, bias=0.0, scale=1.0), "S")
            e.wait(v, "S", sk)
            xc = X[:, i * D:(i + 1) * D]
            vv = e.inc(v.tensor_tensor(xc, xc, TMP[:], ALU.add), "V")
            e.wait(t, "V", vv)
    e.barrier()


def emit_layer3(e, X, XNT, NEGM, RSTD, SS, JUNKF, XNC, IDBF, IDF, MASKU, MASKUB,
                c_sel16, ONES1F, ONESCB, ONESCF, EPSC, EPS9, ROWF, W, w1qt_scr, w2qt_scr,
                dbg=None, ddbg=None):
    nc = e.nc
    t, v, s, q, g = e.t, e.v, e.s, e.q, e.g
    w1scr = e.w1scr3
    w2scr = e.w2scr3
    POLY = [-0.037, 0.481, -2.3, 4.917, -1.791]

    emit_ln(e, X, NEGM, RSTD, JUNKF, SS, EPSC)
    e.barrier()
    with ExitStack() as st:
        PST0 = st.enter_context(nc.psum_tensor("l3_PST0", [128, 128], BF16))
        PST1 = st.enter_context(nc.psum_tensor("l3_PST1", [128, 128], BF16))
        emit_xnt(e, X, XNT, NEGM, RSTD, XNC, PST0, PST1, IDBF)
    e.barrier()

    with ExitStack() as st:
        PF = [st.enter_context(nc.sbuf_tensor(f"p3_PF{b}", [128, 8192], F32)) for b in range(2)]
        PB = [st.enter_context(nc.sbuf_tensor(f"p3_PB{b}", [128, 8192], BF16)) for b in range(2)]
        w1flat = W["L3_ffn_w1"].rearrange("ee r c -> (ee r) c")
        w2flat = W["L3_ffn_w2"].rearrange("ee r c -> (ee r) c")
        jobs = []
        for ex in range(8):
            for b in range(8):
                jobs.append(("w1", w1flat[ex * D + b * 128: ex * D + (b + 1) * 128, :],
                             w1scr[:, ex * 65536 + b * 8192: ex * 65536 + (b + 1) * 8192], 8192))
        for ex in range(8):
            for b2 in range(8):
                jobs.append(("w2",
                             w2flat[ex * 4096 + b2 * 512: ex * 4096 + (b2 + 1) * 512, :]
                             .rearrange("(kt p) c -> p kt c", p=128),
                             w2scr[:, ex * 32768 + b2 * 4096: ex * 32768 + (b2 + 1) * 4096], 4096))
        ld_done = [0, 0]
        cast_done = [0, 0]
        st_done = [0, 0]
        for bi, (kind, src_ap, dst_ap, ncol) in enumerate(jobs):
            bb = bi % 2
            if st_done[bb]:
                e.wait(q, "DG", st_done[bb])
            if kind == "w1":
                ld_done[bb] = e.inc(q.dma_start(out=PF[bb][:, 0:ncol], in_=src_ap), "DH", 16)
            else:
                ld_done[bb] = e.inc(q.dma_start(
                    out=PF[bb][:, 0:ncol].rearrange("p (kt c) -> p kt c", kt=4), in_=src_ap), "DH", 16)
            e.wait(v, "DH", ld_done[bb])
            if st_done[bb]:
                e.wait(v, "DG", st_done[bb])
            cast_done[bb] = e.inc(v.tensor_copy(PB[bb][:, 0:ncol], PF[bb][:, 0:ncol]), "V")
            e.wait(g, "V", cast_done[bb])
            st_done[bb] = e.inc(g.dma_start(out=dst_ap, in_=PB[bb][:, 0:ncol]), "DG", 16)
        e.barrier()

    WMASK = e.sb("f3_WMASK", [128, NCH * 8], F32)
    with ExitStack() as st:
        WGB = st.enter_context(nc.sbuf_tensor("f3_WGB", [128, 64], BF16))
        BGB = st.enter_context(nc.sbuf_tensor("f3_BGB", [128, 8], F32))
        LG = st.enter_context(nc.sbuf_tensor("f3_LG", [128, 8], F32))
        LMP = st.enter_context(nc.sbuf_tensor("f3_LMP", [128, 2], F32))
        PSR = st.enter_context(nc.psum_tensor("f3_PSR", [128, 512], F32))
        e.inc(g.dma_start(out=WGB[:].rearrange("p (ct j) -> p ct j", ct=8),
                          in_=W["L3_ffn_router_w"].rearrange("(ct p) j -> p ct j", p=128),
                          allow_slow_non_contiguous=True), "DG", 16)
        bcast_row(e, BGB, W["L3_ffn_router_b"], ROWF, PSR, ONES1F, 8)
        e.barrier()
        for i in range(NCH):
            for ct in range(CT):
                mm = t.matmul(PSR[:, 0:8], XNT[:, ct * T + i * 128: ct * T + (i + 1) * 128],
                              WGB[:, ct * 8:(ct + 1) * 8], start=(ct == 0), stop=(ct == CT - 1))
            tk = e.inc(mm, "T")
            e.wait(v, "T", tk)
            e.wait(v, "S")
            e.vchain(v.tensor_tensor(LG[:], PSR[:, 0:8], BGB[:], ALU.add))
            e.vchain(v.reduce_max(LMP[:, 0:1], LG[:], axis=AX))
            vv = e.inc(v.tensor_scalar(LMP[:, 1:2], LMP[:, 0:1], -1.0, None, ALU.mult), "V")
            e.wait(s, "V", vv)
            sk = e.inc(s.activation(JUNKF[:, 0:8], LG[:], ACT.Exp,
                                    bias=LMP[:, 1:2], scale=1.0, accum_out=SS[:, 0:1]), "S")
            e.wait(v, "S", sk)
            e.vchain(v.reciprocal(SS[:, 1:2], SS[:, 0:1]))
            e.inc(v.tensor_scalar(LG[:], LG[:], LMP[:, 0:1], None, ALU.is_equal), "V")
            vv = e.inc(v.tensor_scalar(WMASK[:, i * 8:(i + 1) * 8], LG[:],
                                       SS[:, 1:2], None, ALU.mult), "V")
            e.wait(t, "V", vv)
    e.barrier()

    with ExitStack() as st:
        def sb(n, sh, d):
            return st.enter_context(nc.sbuf_tensor(n, sh, d))
        YT = sb("a3_YT", [128, CT * T], BF16)
        with ExitStack() as stm:
            def sbm(n, sh, d):
                return stm.enter_context(nc.sbuf_tensor(n, sh, d))
            VAH = sbm("a3_VAH", [128, NCH * 1056], BF16)
            WROW = sbm("a3_WROW", [128, D], BF16)
            PST0 = stm.enter_context(nc.psum_tensor("a3_PST0", [128, 128], BF16))
            PST1 = stm.enter_context(nc.psum_tensor("a3_PST1", [128, 128], BF16))
            e.inc(v.memset(VAH[:].rearrange("p (a c) -> p a c", c=66)[:, :, 64:65], 1.0), "V")
            e.inc(v.memset(VAH[:].rearrange("p (a c) -> p a c", c=66)[:, :, 65:66], 0.0), "V")
            with ExitStack() as st2:
                WVT = st2.enter_context(nc.sbuf_tensor("a3_WVT", [128, 8 * D], BF16))
                PSA = st2.enter_context(nc.psum_tensor("a3_PSA", [128, 1024], F32))
                load_wT(e, W["L3_attn_Wv"], WROW, WVT, PST0, PST1, IDBF, D)
                e.barrier()
                for i in range(NCH):
                    for nb in range(2):
                        pb = PSA[:, nb * 512: nb * 512 + 512]
                        for ct in range(CT):
                            mm = t.matmul(pb, XNT[:, ct * T + i * 128: ct * T + (i + 1) * 128],
                                          WVT[:, ct * D + nb * 512: ct * D + nb * 512 + 512],
                                          start=(ct == 0), stop=(ct == CT - 1))
                        e.inc(mm, "T")
                    e.wait(v, "T")
                    vv = e.inc(v.tensor_copy(
                        VAH[:, i * 1056:(i + 1) * 1056].rearrange("p (hh c) -> p hh c", c=66)[:, :, 0:64],
                        PSA[:, 0:1024].rearrange("p (hh c) -> p hh c", c=64)), "V")
                    e.wait(t, "V", vv)
            e.barrier()
            KTE = sbm("a3_KTE", [128, T], BF16)
            QTE = sbm("a3_QTE", [128, T], BF16)
            WQTE = sbm("a3_WQTE", [128, D], BF16)
            WKTE = WQTE
            UJ0 = sbm("a3_UJ0", [128, T], BF16)
            UJ1 = sbm("a3_UJ1", [128, T], BF16)
            RZ1 = sbm("a3_RZ1", [128, 16], F32)
            RZT = sbm("a3_RZT", [16, 128], BF16)
            RZ1B = sbm("a3_RZ1B", [128, T], BF16)
            BETA = sbm("a3_BETA", [128, 16], F32)
            BB = sbm("a3_BB", [128, T], BF16)
            PLPB = sbm("a3_PLPB", [128, 512], BF16)
            YNP = sbm("a3_YNP", [128, T], BF16)
            RZ = sbm("a3_RZc", [128, 1], F32)
            EV = sbm("a3_EV", [128, 16], F32)
            EV2 = sbm("a3_EV2", [128, 16], F32)
            SEL16 = sbm("a3_SEL16", [16, T], BF16)
            e.inc(g.dma_start(out=SEL16[:], in_=c_sel16), "DG", 16)
            e.barrier()
            PU0 = stm.enter_context(nc.psum_tensor("a3_PU0", [128, 512], F32))
            PU1 = stm.enter_context(nc.psum_tensor("a3_PU1", [128, 512], F32))
            PMISC = stm.enter_context(nc.psum_tensor("a3_PMISC", [128, 192], F32))
            PSZ = PMISC
            PBB = PMISC[:, 64:192]
            PAV = [stm.enter_context(nc.psum_tensor(f"a3_PAV{b}", [128, 512], F32))
                   for b in range(3)]

            for et in range(8):
                for wnm, dst in (("Wq", QTE), ("Wk", KTE)):
                    wd = W["L3_attn_" + wnm][et * 128:(et + 1) * 128, :]
                    e.wait(g, "V")
                    dk = e.inc(g.dma_start(out=WROW[:], in_=wd), "DG", 16)
                    e.wait(t, "DG", dk)
                    pairs = [(WQTE[:, c * 128:(c + 1) * 128], WROW[:, c * 128:(c + 1) * 128])
                             for c in range(CT)]
                    trans_tiles(e, pairs, PST0, PST1, IDBF)
                    projT(e, dst, [WQTE[:, c * 128:(c + 1) * 128] for c in range(CT)],
                          lambda k, off, n: XNT[:, k * T + off: k * T + off + n],
                          PU_pair(PU0, PU1))
                e.barrier()
                for hh in range(2):
                    h = 2 * et + hh
                    hr = slice(hh * 64, (hh + 1) * 64)

                    def sstrips(j, UJ, extra=None):
                        for sc in range(j // 4, 4):
                            pu = PU0 if sc % 2 == 0 else PU1
                            if sstrips.last_s[sc % 2]:
                                e.wait(t, "S", sstrips.last_s[sc % 2])
                            if sstrips.last_v[sc % 2]:
                                e.wait(t, "V", sstrips.last_v[sc % 2])
                            tk = e.inc(t.matmul(pu[:],
                                                KTE[hr, j * 128:(j + 1) * 128],
                                                QTE[hr, sc * 512:(sc + 1) * 512],
                                                start=True, stop=True), "T")
                            if extra is None:
                                e.wait(s, "T", tk)
                                sstrips.last_s[sc % 2] = e.inc(
                                    s.activation(UJ[:, sc * 512:(sc + 1) * 512], pu[:],
                                                 ACT.Exp, bias=0.0, scale=0.125), "S")
                            else:
                                e.wait(v, "T", tk)
                                e.wait(v, "S")
                                vv = e.inc(v.tensor_tensor(JUNKF[:, 0:512], pu[:],
                                                           extra[:, sc * 512:(sc + 1) * 512],
                                                           ALU.mult), "V")
                                sstrips.last_v[sc % 2] = vv
                                e.wait(s, "V", vv)
                                sstrips.last_s[sc % 2] = e.inc(
                                    s.activation(UJ[:, sc * 512:(sc + 1) * 512], JUNKF[:, 0:512],
                                                 ACT.Exp, bias=0.0, scale=0.125), "S")
                        e.wait(v, "S")
                        vv = e.inc(v.tensor_tensor(UJ[:, j * 128:(j + 1) * 128],
                                                   UJ[:, j * 128:(j + 1) * 128],
                                                   MASKUB[:], ALU.mult), "V")
                        e.wait(t, "V", vv)
                        e.wait(t, "S")

                    sstrips.last_s = [0, 0]
                    sstrips.last_v = [0, 0]
                    last_z = 0
                    for j in range(NCH):
                        UJ = UJ0 if j % 2 == 0 else UJ1
                        if last_z:
                            e.wait(s, "T", last_z)
                        sstrips(j, UJ)
                        for qc in range(j, NCH):
                            mm = t.matmul(PSZ[:, 2 * qc: 2 * qc + 1],
                                          UJ[:, qc * 128:(qc + 1) * 128], ONESCB[:],
                                          start=(j == 0 and qc == 0), stop=(j == qc))
                        last_z = e.inc(mm, "T")
                    e.wait(v, "T", last_z)
                    for qc in range(NCH):
                        e.inc(v.reciprocal(RZ1[:, qc:qc + 1], PSZ[:, 2 * qc:2 * qc + 1]), "V")
                    e.vchain(v.tensor_copy(RZ1[:, 0:1], RZ1[:, 0:1]))
                    e.wait(t, "V")
                    bcast16(e, RZ1, RZT, RZ1B, SEL16, PBB, IDF)
                    sstrips.last_s = [0, 0]
                    sstrips.last_v = [0, 0]
                    last_z = 0
                    for j in range(NCH):
                        UJ = UJ0 if j % 2 == 0 else UJ1
                        if last_z:
                            e.wait(s, "T", last_z)
                        sstrips(j, UJ)
                        for sc in range(j // 4, 4):
                            if last_z:
                                e.wait(v, "T", last_z)
                            vv = e.inc(v.tensor_tensor(JUNKF[:, 0:512],
                                                       UJ[:, sc * 512:(sc + 1) * 512],
                                                       RZ1B[:, sc * 512:(sc + 1) * 512],
                                                       ALU.mult), "V")
                            e.wait(s, "V", vv)
                            sk = e.inc(s.activation(JUNKF[:, 512:1024], JUNKF[:, 0:512],
                                                    ACT.Ln, bias=EPS9[:, 0:1], scale=1.0), "S")
                            e.wait(v, "S", sk)
                            vv = e.inc(v.tensor_tensor(PLPB[:], JUNKF[:, 0:512],
                                                       JUNKF[:, 512:1024], ALU.mult), "V")
                            e.wait(t, "V", vv)
                            for qc in range(max(j, sc * 4), sc * 4 + 4):
                                mm = t.matmul(PSZ[:, 32 + 2 * qc: 33 + 2 * qc],
                                              PLPB[:, (qc - sc * 4) * 128:(qc - sc * 4 + 1) * 128],
                                              ONESCB[:], start=(j == 0 and qc == 0), stop=(j == qc))
                            last_z = e.inc(mm, "T")
                    e.wait(v, "T", last_z)
                    for qc in range(NCH):
                        e.inc(v.tensor_scalar(EV[:, qc:qc + 1], PSZ[:, 32 + 2 * qc:33 + 2 * qc],
                                              -1.0, None, ALU.mult), "V")
                    e.vchain(v.tensor_copy(EV[:], EV[:]))
                    e.vchain(v.tensor_scalar(EV2[:], EV[:], POLY[0], POLY[1], ALU.mult, ALU.add))
                    e.vchain(v.tensor_tensor(EV2[:], EV2[:], EV[:], ALU.mult))
                    e.vchain(v.tensor_scalar(EV2[:], EV2[:], POLY[2], None, ALU.add))
                    e.vchain(v.tensor_tensor(EV2[:], EV2[:], EV[:], ALU.mult))
                    e.vchain(v.tensor_scalar(EV2[:], EV2[:], POLY[3], None, ALU.add))
                    e.vchain(v.tensor_tensor(EV2[:], EV2[:], EV[:], ALU.mult))
                    e.vchain(v.tensor_scalar(EV2[:], EV2[:], POLY[4], 1.0, ALU.add, ALU.max))
                    e.vchain(v.tensor_scalar(EV2[:], EV2[:], -1.0, None, ALU.add))
                    e.vchain(v.tensor_scalar(EV[:], EV[:], 0.5, None, ALU.is_ge))
                    e.vchain(v.tensor_tensor(EV2[:], EV2[:], EV[:], ALU.mult))
                    vv = e.vchain(v.tensor_scalar(BETA[:], EV2[:], 1.0, None, ALU.add))
                    e.wait(t, "V", vv)
                    if dbg == "l3e" and et == 0 and hh == 0:
                        e.barrier()
                        e.inc(q.dma_start(out=ddbg[:, 0:16], in_=RZ1[:]), "DH", 16)
                        e.inc(q.dma_start(out=ddbg[:, 16:32], in_=BETA[:]), "DH", 16)
                        e.inc(q.dma_start(out=ddbg[:, 32:48], in_=EV[:]), "DH", 16)
                        e.inc(g.dma_start(out=ddbg[:, 48:2096], in_=RZ1B[:]), "DG", 16)
                        e.barrier()
                        return
                    bcast16(e, BETA, RZT, BB, SEL16, PBB, IDF)
                    sstrips.last_s = [0, 0]
                    sstrips.last_v = [0, 0]
                    last_av = 0
                    for j in range(NCH):
                        UJ = UJ0 if j % 2 == 0 else UJ1
                        if last_av:
                            e.wait(s, "T", last_av)
                            e.wait(v, "T", last_av)
                        sstrips(j, UJ, extra=BB)
                        for qc in range(j, NCH):
                            bk, sl = (2, qc - 14) if qc >= 14 else (qc // 7, qc % 7)
                            mm = t.matmul(PAV[bk][:, sl * 68: sl * 68 + 66],
                                          UJ[:, qc * 128:(qc + 1) * 128],
                                          VAH[:, j * 1056 + h * 66: j * 1056 + (h + 1) * 66],
                                          start=(j == 0 and qc in (0, 7, 14)), stop=(j == qc))
                        last_av = e.inc(mm, "T")
                        ready = []
                        if j == 6:
                            ready = [(0, qc2) for qc2 in range(0, 7)]
                        elif j == 13:
                            ready = [(1, qc2) for qc2 in range(7, 14)]
                        elif j == 15:
                            ready = [(2, qc2) for qc2 in range(14, 16)]
                        if ready:
                            e.wait(v, "T", last_av)
                            for bk, qc2 in ready:
                                sl = qc2 - (0 if bk == 0 else 7 if bk == 1 else 14)
                                e.vchain(v.reciprocal(RZ[:], PAV[bk][:, sl * 68 + 64: sl * 68 + 65]))
                                vv = e.inc(v.tensor_scalar(
                                    YNP[:, qc2 * 128 + hh * 64: qc2 * 128 + hh * 64 + 64],
                                    PAV[bk][:, sl * 68: sl * 68 + 64], RZ[:, 0:1], None,
                                    ALU.mult), "V")
                            e.wait(t, "V", vv)
                pairs = [(YT[:, et * T + i * 128: et * T + (i + 1) * 128],
                          YNP[:, i * 128:(i + 1) * 128]) for i in range(NCH)]
                trans_tiles(e, pairs, PST0, PST1, IDBF)
            e.barrier()
        with ExitStack() as st2:
            WOT = st2.enter_context(nc.sbuf_tensor("a3_WOT", [128, 8 * D], BF16))
            WROW = st2.enter_context(nc.sbuf_tensor("a3_WROW2", [128, D], BF16))
            PSA = st2.enter_context(nc.psum_tensor("a3_PSA2", [128, 1024], F32))
            PST0 = st2.enter_context(nc.psum_tensor("a3_PST0b", [128, 128], BF16))
            PST1 = st2.enter_context(nc.psum_tensor("a3_PST1b", [128, 128], BF16))
            load_wT(e, W["L3_attn_Wo"], WROW, WOT, PST0, PST1, IDBF, D)
            dk = e.inc(q.dma_start(out=JUNKF[0:1, 0:D], in_=W["L3_attn_bo"][None, :]), "DH", 16)
            e.wait(t, "DH", dk)
            e.barrier()
            for i in range(NCH):
                for nb in range(2):
                    pb = PSA[:, nb * 512: nb * 512 + 512]
                    mm = t.matmul(pb, ONES1F[0:1, :], JUNKF[0:1, nb * 512:(nb + 1) * 512],
                                  start=True, stop=False)
                    for et in range(8):
                        mm = t.matmul(pb, YT[:, et * T + i * 128: et * T + (i + 1) * 128],
                                      WOT[:, et * D + nb * 512: et * D + nb * 512 + 512],
                                      start=False, stop=(et == 7))
                    tk = e.inc(mm, "T")
                e.wait(v, "T", tk)
                vv = e.inc(v.tensor_tensor(X[:, i * D:(i + 1) * D], X[:, i * D:(i + 1) * D],
                                           PSA[:, 0:1024], ALU.add), "V")
                e.wait(t, "V", vv)
    e.barrier()
    if dbg == "l3a":
        for c in range(NCH):
            e.inc(q.dma_start(out=ddbg[:, c * D:(c + 1) * D], in_=X[:, c * D:(c + 1) * D]), "DH", 16)
        e.barrier()
        return

    with ExitStack() as st:
        def sb(n, sh, d):
            return st.enter_context(nc.sbuf_tensor(n, sh, d))
        Z = sb("f3_Z", [128, 32 * 512], BF16)
        W1B0 = sb("f3_W1B0", [128, 8 * 512], BF16)
        W1B1 = sb("f3_W1B1", [128, 8 * 512], BF16)
        W1B2 = sb("f3_W1B2", [128, 8 * 512], BF16)
        W1B3 = sb("f3_W1B3", [128, 8 * 512], BF16)
        W2B0 = sb("f3_W2B0", [128, 4096], BF16)
        W2B1 = sb("f3_W2B1", [128, 4096], BF16)
        SA = sb("f3_SA", [128, 2048], BF16)
        PSF = st.enter_context(nc.psum_tensor("f3_PSF", [128, 4096], F32))

        ab_bufs = [(W1B0, W1B1), (W1B2, W1B3)]
        last_mm1 = [0, 0]
        for th in range(4):
            tof = th * 512
            for ex in range(8):
                for j in range(8):
                    bufa, bufg = ab_bufs[j % 2]
                    if last_mm1[j % 2]:
                        e.wait(q, "T", last_mm1[j % 2])
                    wsl = w1scr[:, ex * 65536:(ex + 1) * 65536].rearrange(
                        "p (ct h) -> p ct h", ct=8)
                    dka = e.inc(q.dma_start(
                        out=bufa[:].rearrange("p (ct c) -> p ct c", ct=8),
                        in_=wsl[:, :, j * 512:(j + 1) * 512]), "DH", 16)
                    dkg = e.inc(q.dma_start(
                        out=bufg[:].rearrange("p (ct c) -> p ct c", ct=8),
                        in_=wsl[:, :, 4096 + j * 512: 4096 + (j + 1) * 512]), "DH", 16)
                    e.wait(t, "DH", dkg)
                    e.wait(t, "S")
                    e.wait(t, "V")
                    for hm in range(4):
                        for ct in range(CT):
                            mm = t.matmul(PSF[:, hm * 512:(hm + 1) * 512],
                                          bufa[:, ct * 512 + hm * 128: ct * 512 + (hm + 1) * 128],
                                          XNT[:, ct * T + tof: ct * T + tof + 512],
                                          start=(ct == 0), stop=(ct == CT - 1))
                    tk = e.inc(mm, "T")
                    e.wait(s, "T", tk)
                    sk = e.inc(s.activation(SA[:], PSF[:, 0:2048], ACT.Silu,
                                            bias=0.0, scale=1.0), "S")
                    e.wait(t, "S", sk)
                    for hm in range(4):
                        for ct in range(CT):
                            mm = t.matmul(PSF[:, hm * 512:(hm + 1) * 512],
                                          bufg[:, ct * 512 + hm * 128: ct * 512 + (hm + 1) * 128],
                                          XNT[:, ct * T + tof: ct * T + tof + 512],
                                          start=(ct == 0), stop=(ct == CT - 1))
                    tk = e.inc(mm, "T")
                    last_mm1[j % 2] = tk
                    e.wait(v, "T", tk)
                    e.wait(v, "S", sk)
                    vv = e.inc(v.tensor_tensor(Z[:, j * 2048:(j + 1) * 2048], SA[:],
                                               PSF[:, 0:2048], ALU.mult), "V")
                    e.wait(t, "V", vv)
                last_q = [0, 0]
                for qd in range(8):
                    wb = W2B0 if qd % 2 == 0 else W2B1
                    if last_q[qd % 2]:
                        e.wait(q, "T", last_q[qd % 2])
                    dk = e.inc(q.dma_start(out=wb[:],
                                           in_=w2scr[:, ex * 32768 + qd * 4096:
                                                     ex * 32768 + (qd + 1) * 4096]), "DH", 16)
                    e.wait(t, "DH", dk)
                    for kt in range(4):
                        ktg = qd * 4 + kt
                        for ch in range(4):
                            for nb in range(2):
                                mm = t.matmul(PSF[:, ch * 1024 + nb * 512: ch * 1024 + nb * 512 + 512],
                                              Z[:, ktg * 512 + ch * 128: ktg * 512 + (ch + 1) * 128],
                                              wb[:, kt * 1024 + nb * 512: kt * 1024 + nb * 512 + 512],
                                              start=(ktg == 0), stop=(ktg == 31))
                    last_q[qd % 2] = e.inc(mm, "T")
                e.wait(v, "T", last_q[qd % 2] if last_q[qd % 2] else None)
                for ch in range(4):
                    ii = th * 4 + ch
                    vv = e.inc(v.scalar_tensor_tensor(
                        X[:, ii * D:(ii + 1) * D], PSF[:, ch * 1024:(ch + 1) * 1024],
                        WMASK[:, ii * 8 + ex: ii * 8 + ex + 1],
                        X[:, ii * D:(ii + 1) * D], ALU.mult, ALU.add), "V")
                e.wait(t, "V", vv)
    e.barrier()


def PU_pair(PU0, PU1):
    class _P:
        def __init__(self):
            self.shape = [128, 1024]

        def __getitem__(self, idx):
            ps, cs = idx
            if cs.start < 512:
                return PU0[ps, cs.start:cs.stop]
            return PU1[ps, cs.start - 512:cs.stop - 512]
    return _P()


def bcast16(e, COL, RZT, DSTB, SEL16, PBB, IDF):
    nc = e.nc
    t, v = e.t, e.v
    e.wait(t, "V")
    tk = e.inc(t.transpose(PBB[0:16, 0:128], COL[:], IDF[:]), "T")
    e.wait(v, "T", tk)
    vv = e.inc(v.tensor_copy(RZT[:], PBB[0:16, 0:128]), "V")
    e.wait(t, "V", vv)
    for qc in range(16):
        tk = e.inc(t.matmul(PBB[:, 0:128], SEL16[0:16, qc * 128:(qc + 1) * 128],
                            RZT[0:16, :], start=True, stop=True), "T")
        e.wait(v, "T", tk)
        vv = e.inc(v.tensor_copy(DSTB[:, qc * 128:(qc + 1) * 128], PBB[:, 0:128]), "V")
        e.wait(t, "V", vv)


def make_consts():
    r = np.arange(128)
    masku = (r[:, None] <= r[None, :]).astype(np.float32)
    sel16 = np.zeros((16, 2048), np.float32)
    for qc in range(16):
        sel16[qc, qc * 128:(qc + 1) * 128] = 1.0
    return {
        "c_idbf": np.eye(128, dtype=ml_dtypes.bfloat16),
        "c_idf": np.eye(128, dtype=np.float32),
        "c_masku": masku,
        "c_sel16": sel16,
    }


def kernel(**inputs):
    nc = build_nc(4)
    x = np.ascontiguousarray(inputs["L0_attn_x"], dtype=np.float32)
    base = {k: np.ascontiguousarray(v) for k, v in inputs.items() if k != "L0_attn_x"}
    base.update(make_consts())
    in_maps = []
    for c in range(8):
        m = dict(base)
        m["x_in"] = np.ascontiguousarray(x[c])
        in_maps.append(m)
    res = run_bass_kernel_spmd(nc, in_maps, core_ids=list(range(8)))
    return np.stack([r["out"] for r in res.results], axis=0).astype(np.float32)
